# version 12
# baseline (speedup 1.0000x reference)
import numpy as np
import ml_dtypes
import concourse.bass as bass
import concourse.mybir as mybir
from concourse.bass_utils import run_bass_kernel_spmd

F32 = mybir.dt.float32
BF16 = mybir.dt.bfloat16
I32 = mybir.dt.int32
AF = mybir.ActivationFunctionType
ALU = mybir.AluOpType
AX = mybir.AxisListType

NCORES = 8
B, S, D = 16, 2048, 2048
NH, HD = 16, 128
INTER, ICC = 5504, 43
SEQ_PER_CORE = B // NCORES
TC = S // 128
SCALE = 1.0 / float(np.sqrt(HD))
EPS = 1e-5
WALL_ROWS = 3 * ICC * 128
SHARD_ROWS = WALL_ROWS // NCORES
DEC_UNITS = ((SHARD_ROWS + 127) // 128) * 4

DBG = False
LIMIT = ""

ENGINES = ("tensor", "vector", "scalar", "gpsimd", "sync")
ESEM = {"tensor": "pe", "vector": "dve", "scalar": "act", "gpsimd": "pool", "sync": "syncq"}


class Plan:
    def __init__(self):
        self.steps = {e: [] for e in ENGINES}
        self.cnt = {"pe": 0, "dve": 0, "act": 0, "pool": 0, "syncq": 0, "cc": 0}
        self.bufs = {}
        self.water = {e: {} for e in ENGINES}
        self.dummy_fns = None

    def _buf(self, k):
        if k not in self.bufs:
            self.bufs[k] = {"w": None, "r": {}}
        return self.bufs[k]

    def _need(self, engine, dep):
        s, c, is_async = dep
        if s == ESEM[engine] and not is_async:
            return
        if s == "pe" and engine != "tensor":
            if self.cnt["pe"] >= c + 2:
                c = c + 2
            else:
                self.cnt["pe"] += 1
                self.steps["tensor"].append(("ops", [
                    (self.dummy_fns[0], 0),
                    (self.dummy_fns[1], 1),
                ], "pe"))
                c = self.cnt["pe"]
        if self.water[engine].get(s, -1) >= c:
            return
        self.steps[engine].append(("wait", s, c))
        self.water[engine][s] = c

    def g(self, engine, ops, reads=(), writes=(), sem=None, dma=False):
        if dma:
            hot = False
            for k in reads:
                b = self._buf(k)
                if b["w"] and b["w"][0] in ("dve", "act", "pool"):
                    hot = True
            if hot:
                ops = [(lambda e: e.nop(cycle_cnt=250, nofuse=True), 0)] * 4 + list(ops)
        for k in reads:
            b = self._buf(k)
            if b["w"]:
                self._need(engine, b["w"])
        for k in writes:
            b = self._buf(k)
            if b["w"]:
                self._need(engine, b["w"])
            for s, (c, ia) in list(b["r"].items()):
                self._need(engine, (s, c, ia))
        semn = sem or ESEM[engine]
        tot = sum(i for _, i in ops)
        self.cnt[semn] += tot
        is_async = dma or semn == "cc"
        mile = (semn, self.cnt[semn], is_async)
        self.steps[engine].append(("ops", ops, semn))
        for k in writes:
            b = self._buf(k)
            b["w"] = mile
            b["r"] = {}
        for k in reads:
            b = self._buf(k)
            cur = b["r"].get(semn, (-1, False))
            if mile[1] > cur[0]:
                b["r"][semn] = (mile[1], is_async)
        return mile

    def barrier(self):
        snap = dict(self.cnt)
        for e in ENGINES:
            for s in ("pe", "dve", "act", "pool", "syncq", "cc"):
                if snap[s] > 0:
                    self._need(e, (s, snap[s], True))

    def emit(self, engine, eng, semh):
        for st in self.steps[engine]:
            if st[0] == "wait":
                eng.wait_ge(semh[st[1]], st[2])
            else:
                _, ops, semn = st
                for fn, inc in ops:
                    inst = fn(eng)
                    if inc:
                        inst.then_inc(semh[semn], inc)


def build_nc():
    nc = bass.Bass("TRN2", target_bir_lowering=False)
    p = Plan()

    x_in = nc.dram_tensor("x", [SEQ_PER_CORE, S, D], F32, kind="ExternalInput").ap()
    y_out = nc.dram_tensor("y", [SEQ_PER_CORE, S, D], F32, kind="ExternalOutput").ap()
    wq_d = nc.dram_tensor("wq_t", [NH, 128, 16, 128], F32, kind="ExternalInput").ap()
    wk_d = nc.dram_tensor("wk_t", [NH, 128, 16, 128], F32, kind="ExternalInput").ap()
    wv_d = nc.dram_tensor("wv_t", [NH // 2, 128, 16, 256], F32, kind="ExternalInput").ap()
    wo_d = nc.dram_tensor("wo_t", [4, 128, 16, 512], F32, kind="ExternalInput").ap()
    lut_d = nc.dram_tensor("lut", [16], F32, kind="ExternalInput").ap()
    idx_d = nc.dram_tensor("idx_shard", [SHARD_ROWS, 2048], I32, kind="ExternalInput").ap()
    ident_d = nc.dram_tensor("ident", [128, 128], F32, kind="ExternalInput").ap()

    shard = nc.dram_tensor("shard", [SHARD_ROWS, 2048], BF16, kind="Internal")
    wall = nc.dram_tensor("wall", [WALL_ROWS, 2048], BF16, kind="Internal")
    _sk = "ExternalOutput" if DBG else "Internal"
    ys = nc.dram_tensor("ys", [S, D], F32, kind=_sk).ap()
    x2s = nc.dram_tensor("x2s", [S, D], F32, kind=_sk).ap()
    hs = nc.dram_tensor("hs", [4 * 128, ICC * 512], BF16, kind=_sk).ap()
    wall_ap = wall.ap()
    if DBG:
        dxn1 = nc.dram_tensor("dxn1", [128, 16 * 2048], BF16, kind="ExternalOutput").ap()
        dxn2 = nc.dram_tensor("dxn2", [128, 16 * 2048], BF16, kind="ExternalOutput").ap()
        dq = nc.dram_tensor("dq", [128, 2048], BF16, kind="ExternalOutput").ap()
        dk = nc.dram_tensor("dk", [128, 2048], BF16, kind="ExternalOutput").ap()
        dv4 = nc.dram_tensor("dv4", [128, 4096], BF16, kind="ExternalOutput").ap()
        dtiny = nc.dram_tensor("dtiny", [128, 16], F32, kind="ExternalOutput").ap()
        dao = nc.dram_tensor("dao", [128, 16 * 2048], BF16, kind="ExternalOutput").ap()
        dwall = nc.dram_tensor("dwall", [256, 2048], BF16, kind="ExternalOutput").ap()
        dcs = nc.dram_tensor("dcs", [128, 64], F32, kind="ExternalOutput").ap()
        dybl = nc.dram_tensor("dybl", [4, 128, 2048], F32, kind="ExternalOutput").ap()
        dx2e = nc.dram_tensor("dx2e", [4, 128, 2048], F32, kind="ExternalOutput").ap()

    all_ctx = []

    def mk_scope():
        return []

    def sb(scope, name, shape, dt):
        gg = nc.sbuf_tensor(name, shape, dt)
        t = gg.__enter__()
        scope.append(gg)
        return t

    def close_scope(scope):
        for gg in reversed(scope):
            gg.__exit__(None, None, None)

    def ps_alloc(scope, name):
        gg = nc.psum_tensor(name, [128, 512], F32)
        t = gg.__enter__()
        scope.append(gg)
        return t

    persist = mk_scope()
    XN = sb(persist, "XN", [128, 16 * 2048], BF16)
    ident = sb(persist, "ident_s", [128, 128], F32)
    ones_f = sb(persist, "ones_f", [128, 1], F32)
    ones_bf = sb(persist, "ones_bf", [128, 1], BF16)
    ones_row = sb(persist, "ones_row", [1, 128], F32)
    lut_s = sb(persist, "lut_sb", [128, 16], F32)
    delta_s = sb(persist, "delta_sb", [128, 16], F32)
    stats = sb(persist, "stats", [128, 32], F32)
    cstat = sb(persist, "cstat", [128, 64], F32)
    csq = sb(persist, "csq", [128, 64], F32)
    tiny = sb(persist, "tiny", [128, 16], F32)
    row_s = sb(persist, "row_s", [1, 520], F32)
    bc_sb = sb(persist, "bc_sb", [128, 512], F32)
    trash = sb(persist, "trash", [128, 2048], BF16)
    xch = [sb(persist, "xch0", [128, 2048], F32)]
    dum_rhs = sb(persist, "dum_rhs", [128, 256], BF16)
    PS = [ps_alloc(persist, f"ps{i}") for i in range(8)]
    p.dummy_fns = (
        lambda e: e.matmul(PS[7][0:1, 0:256], ones_bf[:, :], dum_rhs[:, :],
                           start=True, stop=True),
        lambda e: e.matmul(PS[7][0:1, 256:512], ones_bf[:, :], dum_rhs[:, :],
                           start=True, stop=True),
    )

    def pe_settle(keys, n=16):
        ops = []
        for i in range(n):
            ops.append((p.dummy_fns[i % 2], 0))
        ops[-1] = (ops[-1][0], 1)
        p.g("tensor", ops, reads=list(keys), writes=[("ps", 7)])

    def mm(out_ap, lhs_ap, rhs_ap, start, stop):
        return lambda e: e.matmul(out_ap, lhs_ap, rhs_ap, start=start, stop=stop)

    CONSTS = [0.0, 1.0, -1.0, 0.5, -0.5, EPS, 1e-20, SCALE, -SCALE, 1.0 / D]
    cvals = sb(persist, "cvals", [128, len(CONSTS)], F32)
    for i, v in enumerate(CONSTS):
        nc.const_aps.register(F32, v, cvals[:, i:i + 1])

    init_ops = [
        (lambda e: e.memset(dum_rhs[:, :], 0.0), 0),
        (lambda e: e.memset(ones_f[:, :], 1.0), 0),
        (lambda e: e.memset(ones_bf[:, :], 1.0), 0),
        (lambda e: e.memset(ones_row[:, :], 1.0), 0),
    ]
    for i, v in enumerate(CONSTS):
        init_ops.append((lambda e, i=i, v=v: e.memset(cvals[:, i:i + 1], v), 0))
    init_ops[-1] = (init_ops[-1][0], 1)
    p.g("vector", init_ops, writes=[("ones",)])
    p.g("sync", [
        (lambda e: e.dma_start(out=ident[:, :], in_=ident_d), 16),
        (lambda e: e.dma_start(out=lut_s[:, :], in_=lut_d.unsqueeze(0).partition_broadcast(128)), 16),
    ], writes=[("ident",), ("lut",)], dma=True)
    p.g("vector", [
        (lambda e: e.tensor_tensor(delta_s[:, 1:16], lut_s[:, 1:16], lut_s[:, 0:15], ALU.subtract), 1),
    ], reads=[("lut",)], writes=[("delta",)])
    p.barrier()

    def plan_rowln(src_t, src_key, work_t, work_key, c0, slotkey):
        p.g("scalar", [
            (lambda e: e.activation(trash[:, 0:src_t.shape[1]], src_t[:, :], AF.Copy,
                                    bias=0.0, scale=1.0,
                                    accum_out=stats[:, c0:c0 + 1]), 1),
        ], reads=[src_key], writes=[("stats", slotkey), ("trash",), ("trashrow", 0),
                                    ("trashrow", 1), ("trashrow", 2), ("trashrow", 3)])
        p.g("gpsimd", [
            (lambda e: e.tensor_scalar(stats[:, c0:c0 + 1], stats[:, c0:c0 + 1],
                                       1.0 / D, None, ALU.mult), 1),
        ], reads=[("stats", slotkey)], writes=[("stats", slotkey)])
        p.g("vector", [
            (lambda e: e.tensor_scalar(work_t[:, :], src_t[:, :], stats[:, c0:c0 + 1],
                                       1.0, ALU.subtract, ALU.mult), 1),
        ], reads=[src_key, ("stats", slotkey)], writes=[work_key])
        p.g("scalar", [
            (lambda e: e.activation(trash[:, 0:work_t.shape[1]], work_t[:, :], AF.Square,
                                    bias=0.0, scale=1.0,
                                    accum_out=stats[:, c0 + 1:c0 + 2]), 0),
            (lambda e: e.activation(stats[:, c0 + 2:c0 + 3], stats[:, c0 + 1:c0 + 2],
                                    AF.Ln, bias=EPS, scale=1.0 / D), 1),
        ], reads=[work_key, ("stats", slotkey)], writes=[("stats", slotkey), ("trash",),
                                    ("trashrow", 0), ("trashrow", 1), ("trashrow", 2), ("trashrow", 3)])
        p.g("gpsimd", [
            (lambda e: e.tensor_scalar(stats[:, c0 + 3:c0 + 4], stats[:, c0 + 2:c0 + 3],
                                       -0.5, None, ALU.mult), 1),
        ], reads=[("stats", slotkey)], writes=[("stats", slotkey)])
        p.g("scalar", [
            (lambda e: e.activation(stats[:, c0 + 3:c0 + 4], stats[:, c0 + 3:c0 + 4],
                                    AF.Exp, bias=0.0, scale=1.0), 1),
        ], reads=[("stats", slotkey)], writes=[("stats", slotkey)])

    def plan_ln_to_xn(seq, xnt):
        for tc in range(TC):
            slot = tc % 2
            xc = xch[0]
            src = x_in[seq, tc * 128:(tc + 1) * 128, :]
            p.g("sync", [(lambda e, xc=xc, src=src: e.dma_start(out=xc[:, :], in_=src), 16)],
                writes=[("xch", 0)], dma=True)
            c0 = 4 * slot
            plan_rowln(xc, ("xch", 0), xnt, ("xnt",), c0, slot)
            p.g("vector", [
                (lambda e, xc=xc, c0=c0: e.tensor_scalar(
                    xc[:, :], xnt[:, :], stats[:, c0 + 3:c0 + 4], 1.0,
                    ALU.mult, ALU.mult), 1),
            ], reads=[("xnt",), ("stats", slot)], writes=[("xch", 0)])
            for g4 in range(4):
                bank = PS[g4 % 2]
                tr_ops = []
                for j in range(4):
                    dinc = g4 * 4 + j
                    tr_ops.append((lambda e, xc=xc, dinc=dinc, j=j, bank=bank: e.transpose(
                        bank[:, j * 128:(j + 1) * 128],
                        xc[:, dinc * 128:(dinc + 1) * 128], ident[:, :]), 0))
                tr_ops[-1] = (tr_ops[-1][0], 1)
                p.g("tensor", tr_ops, reads=[("xch", 0), ("ident",)], writes=[("ps", g4 % 2)])
                xnap = XN[:, :].rearrange("p (dinc t) -> p dinc t", dinc=16)
                p.g("scalar", [
                    (lambda e, g4=g4, tc=tc, bank=bank, xnap=xnap: e.activation(
                        xnap[:, g4 * 4:g4 * 4 + 4, tc * 128:(tc + 1) * 128],
                        bank[:, :], AF.Copy, bias=0.0, scale=1.0), 1),
                ], reads=[("ps", g4 % 2)], writes=[("XN", tc)])

    for seq in range(SEQ_PER_CORE):
        attn_sc = mk_scope()
        AO = sb(attn_sc, f"AO{seq}", [128, 16 * 2048], BF16)
        ab_sc = mk_scope()
        xnt = sb(ab_sc, f"xnt{seq}", [128, 2048], F32)
        WQh = sb(ab_sc, f"WQh{seq}", [128, 2048], BF16)
        WKh = sb(ab_sc, f"WKh{seq}", [128, 2048], BF16)
        WVg = sb(ab_sc, f"WVg{seq}", [128, 16 * 256], BF16)
        Qh = sb(ab_sc, f"Qh{seq}", [128, 2048], BF16)
        Kh = sb(ab_sc, f"Kh{seq}", [128, 2048], BF16)
        Vg = sb(ab_sc, f"Vg{seq}", [128, 16 * 256], BF16)
        PT = [sb(ab_sc, f"PT{seq}_{i}", [128, 512], BF16) for i in range(2)]
        PTM = [sb(ab_sc, f"PTM{seq}_{i}", [128, 512], BF16) for i in range(2)]
        S_acc = sb(ab_sc, f"S_acc{seq}", [128, 512], F32)
        

        if seq == 0:
            dec_sc = mk_scope()
            IDXT = [sb(dec_sc, f"IDXT{i}", [128, 512], BF16) for i in range(2)]
            DACC = sb(dec_sc, "DACC", [128, 512], F32)
            DACP = sb(dec_sc, "DACP", [128, 512], F32)
            DTRM = [sb(dec_sc, f"DTRM{i}", [128, 512], F32) for i in range(2)]
            WOUT = [sb(dec_sc, f"WOUT{i}", [128, 512], BF16) for i in range(2)]
            dec_state = {"i": 0}

            def plan_decode_unit():
                u = dec_state["i"]
                if u >= DEC_UNITS:
                    return
                dec_state["i"] += 1
                rc, cq = divmod(u, 4)
                r0 = rc * 128
                rows = min(128, SHARD_ROWS - r0)
                c0_ = cq * 512
                slot = u % 2
                it = IDXT[slot]
                wo_ = WOUT[slot]
                p.g("gpsimd", [
                    (lambda e, it=it, r0=r0, rows=rows, c0_=c0_: e.dma_start(
                        out=it[0:rows, :], in_=idx_d[r0:r0 + rows, c0_:c0_ + 512]), 16),
                ], writes=[("IDXT", slot)], dma=True)
                dve_ops = []
                for v in range(1, 9):
                    ts = DTRM[0]
                    if v == 1:
                        dve_ops.append((lambda e, v=v, it=it, rows=rows: e.tensor_scalar(
                            DACC[0:rows, :], it[0:rows, :], float(v) - 0.5,
                            delta_s[0:rows, v:v + 1], ALU.is_ge, ALU.mult), 0))
                    else:
                        dve_ops.append((lambda e, v=v, it=it, ts=ts, rows=rows: e.tensor_scalar(
                            ts[0:rows, :], it[0:rows, :], float(v) - 0.5,
                            delta_s[0:rows, v:v + 1], ALU.is_ge, ALU.mult), 0))
                        dve_ops.append((lambda e, ts=ts, rows=rows: e.tensor_tensor(
                            DACC[0:rows, :], DACC[0:rows, :], ts[0:rows, :], ALU.add), 0))
                dve_ops[-1] = (dve_ops[-1][0], 1)
                p.g("vector", dve_ops, reads=[("IDXT", slot), ("delta",)],
                    writes=[("DACC",), ("DTRM", 0)])
                for v in range(9, 16):
                    ts = DTRM[v % 2]
                    p.g("vector", [
                        (lambda e, v=v, it=it, ts=ts, rows=rows: e.tensor_scalar(
                            ts[0:rows, :], it[0:rows, :], float(v) - 0.5,
                            delta_s[0:rows, v:v + 1], ALU.is_ge, ALU.mult), 1),
                    ], reads=[("IDXT", slot), ("delta",)], writes=[("DTRM", v % 2)])
                    if v == 9:
                        p.g("gpsimd", [
                            (lambda e, ts=ts, rows=rows: e.tensor_copy(
                                DACP[0:rows, :], ts[0:rows, :]), 1),
                        ], reads=[("DTRM", v % 2)], writes=[("DACP",)])
                    else:
                        p.g("gpsimd", [
                            (lambda e, ts=ts, rows=rows: e.tensor_tensor(
                                DACP[0:rows, :], DACP[0:rows, :], ts[0:rows, :], ALU.add), 1),
                        ], reads=[("DTRM", v % 2)], writes=[("DACP",)])
                p.g("vector", [
                    (lambda e, wo_=wo_, rows=rows: e.scalar_tensor_tensor(
                        wo_[0:rows, :], DACC[0:rows, :], lut_s[0:rows, 0:1], DACP[0:rows, :],
                        ALU.add, ALU.add), 1),
                ], reads=[("DACC",), ("DACP",), ("lut",)], writes=[("WOUT", slot)])
                p.g("sync", [
                    (lambda e, wo_=wo_, r0=r0, rows=rows, c0_=c0_: e.dma_start(
                        out=shard.ap()[r0:r0 + rows, c0_:c0_ + 512], in_=wo_[0:rows, :]), 16),
                ], reads=[("WOUT", slot)], writes=[("shard_part", u)], dma=True)
        else:
            def plan_decode_unit():
                pass

        def wload(h):
            if h >= NH:
                return
            p.g("gpsimd", [
                (lambda e, h=h: e.dma_start(out=WQh[:, :], in_=wq_d[h]), 16),
                (lambda e, h=h: e.dma_start(out=WKh[:, :], in_=wk_d[h]), 16),
            ], writes=[("WQh",), ("WKh",)], dma=True)
            if h % 2 == 0:
                g_ = h // 2
                p.g("gpsimd", [
                    (lambda e, g_=g_: e.dma_start(out=WVg[:, :], in_=wv_d[g_]), 16),
                ], writes=[("WVg",)], dma=True)

        xn_reads = [("XN", t) for t in range(TC)]

        def plan_A(h):
            if not _first_settle["done"]:
                _first_settle["done"] = True
                pe_settle([("WQh",), ("WKh",), ("WVg",)])
            if h % 2 == 0:
                for kc in range(16):
                    bank = PS[2 + kc % 2]
                    ops = []
                    for dinc in range(16):
                        ops.append((mm(bank[:, 0:256],
                                       XN[:, dinc * 2048 + kc * 128: dinc * 2048 + (kc + 1) * 128],
                                       WVg[:, dinc * 256:(dinc + 1) * 256],
                                       dinc == 0, dinc == 15), 0))
                    ops[-1] = (ops[-1][0], 1)
                    p.g("tensor", ops, reads=xn_reads + [("WVg",)], writes=[("ps", 2 + kc % 2)])
                    p.g("scalar", [
                        (lambda e, kc=kc, bank=bank: e.activation(
                            Vg[:, kc * 256:(kc + 1) * 256], bank[:, 0:256],
                            AF.Copy, bias=0.0, scale=1.0), 1),
                    ], reads=[("ps", 2 + kc % 2)], writes=[("Vg", kc)])
            for (wt, wkey, dst, key) in ((WQh, "WQh", Qh, "Qh"), (WKh, "WKh", Kh, "Kh")):
                for tb in range(4):
                    bank = PS[2 + tb % 2]
                    ops = []
                    for dinc in range(16):
                        ops.append((mm(bank[:, :],
                                       wt[:, dinc * 128:(dinc + 1) * 128],
                                       XN[:, dinc * 2048 + tb * 512: dinc * 2048 + (tb + 1) * 512],
                                       dinc == 0, dinc == 15), 0))
                    ops[-1] = (ops[-1][0], 1)
                    p.g("tensor", ops, reads=xn_reads + [(wkey,)], writes=[("ps", 2 + tb % 2)])
                    p.g("scalar", [
                        (lambda e, dst=dst, tb=tb, bank=bank: e.activation(
                            dst[:, tb * 512:(tb + 1) * 512], bank[:, :],
                            AF.Copy, bias=0.0, scale=1.0), 1),
                    ], reads=[("ps", 2 + tb % 2)], writes=[(key, tb)])
            for (src, srck, col) in ((Kh, "Kh", 8), (Qh, "Qh", 9)):
                p.g("scalar", [
                    (lambda e, src=src: e.activation(trash[:, :], src[:, :], AF.Square,
                                                     bias=0.0, scale=1.0), 1),
                ], reads=[(srck, t) for t in range(4)],
                    writes=[("trash",), ("trashrow", 0), ("trashrow", 1),
                            ("trashrow", 2), ("trashrow", 3)])
                for tb in range(4):
                    p.g("tensor", [
                        (mm(PS[6][0:1, :], ones_bf[:, :], trash[:, tb * 512:(tb + 1) * 512],
                            True, True), 1),
                    ], reads=[("trash",), ("ones",)], writes=[("ps", 6)])
                    p.g("vector", [
                        (lambda e, tb=tb: e.tensor_copy(
                            trash[0:1, tb * 512:(tb + 1) * 512], PS[6][0:1, :]), 1),
                    ], reads=[("ps", 6)], writes=[("trashrow", tb)])
                p.g("vector", [
                    (lambda e, col=col: e.tensor_reduce(
                        tiny[0:1, col:col + 1], trash[0:1, :], AX.X, ALU.max), 1),
                ], reads=[("trashrow", t) for t in range(4)], writes=[("tiny", col)])
            p.g("scalar", [
                (lambda e: e.activation(tiny[0:1, 10:11], tiny[0:1, 8:9], AF.Ln,
                                        bias=1e-20, scale=1.0), 0),
                (lambda e: e.activation(tiny[0:1, 11:12], tiny[0:1, 9:10], AF.Ln,
                                        bias=1e-20, scale=1.0), 1),
            ], reads=[("tiny", 8), ("tiny", 9)], writes=[("tiny", 10), ("tiny", 11)])
            p.g("gpsimd", [
                (lambda e: e.tensor_tensor(tiny[0:1, 12:13], tiny[0:1, 10:11],
                                           tiny[0:1, 11:12], ALU.add), 1),
            ], reads=[("tiny", 10), ("tiny", 11)], writes=[("tiny", 12)])
            p.g("vector", [
                (lambda e: e.tensor_scalar(tiny[0:1, 13:14], tiny[0:1, 12:13],
                                           0.5, None, ALU.mult), 1),
            ], reads=[("tiny", 12)], writes=[("tiny", 13)])
            p.g("scalar", [
                (lambda e: e.activation(tiny[0:1, 14:15], tiny[0:1, 13:14], AF.Exp,
                                        bias=0.0, scale=1.0), 1),
            ], reads=[("tiny", 13)], writes=[("tiny", 14)])
            p.g("gpsimd", [
                (lambda e: e.tensor_scalar(tiny[0:1, 12:13], tiny[0:1, 14:15], -SCALE, None,
                                           ALU.mult), 1),
            ], reads=[("tiny", 14)], writes=[("tiny", 12)])
            p.g("tensor", [
                (mm(PS[6][:, 0:1], ones_row[:, :], tiny[0:1, 12:13], True, True), 1),
            ], reads=[("tiny", 12), ("ones",)], writes=[("ps", 6)])
            p.g("scalar", [
                (lambda e: e.activation(tiny[:, 15:16], PS[6][:, 0:1],
                                        AF.Copy, bias=0.0, scale=1.0), 1),
            ], reads=[("ps", 6)], writes=[("negb",)])

        def plan_B(h):
            wload(h + 1)
            hh = h % 2
            for qb in range(4):
                kc_full = 4 * qb
                kc_max = min(4 * qb + 4, 15)
                for kc in range(kc_max + 1):
                    scb = kc % 2
                    p.g("tensor", [
                        (mm(PS[scb][:, :], Kh[:, kc * 128:(kc + 1) * 128],
                            Qh[:, qb * 512:(qb + 1) * 512], True, True), 1),
                    ], reads=[("Kh", kc // 4), ("Qh", qb)], writes=[("ps", scb)])
                    p.g("scalar", [
                        (lambda e, scb=scb: e.activation(
                            PT[scb][:, :], PS[scb][:, :], AF.Exp,
                            bias=tiny[:, 15:16], scale=SCALE), 1),
                    ], reads=[("ps", scb), ("negb",)], writes=[("PT", scb)])
                    src_pt = PT[scb]
                    ptkey = ("PT", scb)
                    if kc > kc_full:
                        base = qb * 512 - kc * 128 + 128
                        p.g("gpsimd", [
                            (lambda e, scb=scb, base=base: e.affine_select(
                                PTM[scb][:, :], PT[scb][:, :], [[1, 512]], ALU.is_ge,
                                0.0, base=base, channel_multiplier=-1), 1),
                        ], reads=[("PT", scb)], writes=[("PTM", scb)])
                        src_pt = PTM[scb]
                        ptkey = ("PTM", scb)
                    if kc == 0:
                        p.g("vector", [
                            (lambda e, src_pt=src_pt: e.tensor_copy(S_acc[:, :], src_pt[:, :]), 1),
                        ], reads=[ptkey], writes=[("S_acc",)])
                    else:
                        p.g("vector", [
                            (lambda e, src_pt=src_pt: e.tensor_tensor(
                                S_acc[:, :], S_acc[:, :], src_pt[:, :], ALU.add), 1),
                        ], reads=[ptkey], writes=[("S_acc",)])
                    p.g("tensor", [
                        (mm(PS[4][:, :],
                            Vg[:, kc * 256 + hh * 128: kc * 256 + hh * 128 + 128],
                            src_pt[:, :], kc == 0, kc == kc_max), 1),
                    ], reads=[("Vg", kc), ptkey], writes=[("ps", 4)])
                p.g("tensor", [
                    (mm(PS[5][0:1, :], ones_f[:, :], S_acc[:, :], True, True), 1),
                ], reads=[("S_acc",), ("ones",)], writes=[("ps", 5)])
                p.g("vector", [
                    (lambda e: e.reciprocal(row_s[0:1, 0:512], PS[5][0:1, :]), 1),
                ], reads=[("ps", 5)], writes=[("row0",)])
                p.g("tensor", [
                    (mm(PS[5][:, :], ones_row[:, :], row_s[0:1, 0:512], True, True), 1),
                ], reads=[("row0",), ("ones",)], writes=[("ps", 5)])
                p.g("scalar", [
                    (lambda e: e.activation(bc_sb[:, :], PS[5][:, :], AF.Copy,
                                            bias=0.0, scale=1.0), 1),
                ], reads=[("ps", 5)], writes=[("bc",)])
                p.g("vector", [
                    (lambda e, h=h, qb=qb: e.scalar_tensor_tensor(
                        AO[:, h * 2048 + qb * 512: h * 2048 + (qb + 1) * 512],
                        PS[4][:, :], 1.0, bc_sb[:, :], ALU.mult, ALU.mult), 1),
                ], reads=[("ps", 4), ("bc",)], writes=[("AO", h, qb)])

        plan_ln_to_xn(seq, xnt)
        _first_settle = {"done": False}
        if DBG and seq == 0:
            p.g("sync", [(lambda e: e.dma_start(out=dxn1, in_=XN[:, :]), 16)],
                reads=[("XN", t) for t in range(TC)], dma=True)
            p.g("sync", [(lambda e: e.dma_start(out=dcs, in_=cstat[:, :]), 16)],
                reads=[("cstat", t, 0) for t in range(TC)], dma=True)
        if LIMIT == "ln1":
            if seq == 0:
                close_scope(dec_sc)
            close_scope(ab_sc)
            close_scope(attn_sc)
            break
        wload(0)
        for h in range(NH):
            plan_A(h)
            if DBG and seq == 0 and h == 0:
                p.g("sync", [
                    (lambda e: e.dma_start(out=dq, in_=Qh[:, :]), 16),
                    (lambda e: e.dma_start(out=dk, in_=Kh[:, :]), 16),
                    (lambda e: e.dma_start(out=dv4, in_=Vg[:, :]), 16),
                    (lambda e: e.dma_start(out=dtiny, in_=tiny[:, :]), 16),
                ], reads=[("Qh", 0), ("Qh", 1), ("Qh", 2), ("Qh", 3),
                          ("Kh", 0), ("Kh", 1), ("Kh", 2), ("Kh", 3)] +
                         [("Vg", kc) for kc in range(16)] + [("negb",)], dma=True)
            plan_B(h)
            for _ in range(4):
                plan_decode_unit()
        if seq == 0:
            while dec_state["i"] < DEC_UNITS:
                plan_decode_unit()
            rds = [("shard_part", i) for i in range(DEC_UNITS)]
            p.g("gpsimd", [
                (lambda e: e.collective_compute(
                    "AllGather", ALU.bypass, [list(range(NCORES))],
                    [shard.ap().opt()], [wall_ap.opt()]), 1),
            ], reads=rds, writes=[("wall",)], sem="cc")
            close_scope(dec_sc)
        if DBG and seq == 0:
            p.g("sync", [(lambda e: e.dma_start(out=dao, in_=AO[:, :]), 16)],
                reads=[("AO", h, qb) for h in range(NH) for qb in range(4)], dma=True)
            p.g("sync", [(lambda e: e.dma_start(out=dwall, in_=wall_ap[2 * ICC * 128:2 * ICC * 128 + 256, :]), 16)],
                reads=[("wall",)], dma=True)
        close_scope(ab_sc)
        p.barrier()

        c_sc = mk_scope()
        WOr = sb(c_sc, f"WOr{seq}", [128, 16 * 512], BF16)
        ytile = [sb(c_sc, f"ytile{seq}_{i}", [128, 512], F32) for i in range(2)]
        yfull = [sb(c_sc, f"yfull{seq}_0", [128, 2048], F32)]
        x2t = [sb(c_sc, f"x2t{seq}_{i}", [128, 2048], F32) for i in range(2)]
        tmpf = sb(c_sc, f"tmpf{seq}", [128, 2048], F32)
        xnt2 = tmpf

        ao_reads = [("AO", h, qb) for h in range(NH) for qb in range(4)]
        for q in range(4):
            p.g("gpsimd", [
                (lambda e, q=q: e.dma_start(out=WOr[:, :], in_=wo_d[q]), 16),
            ], writes=[("WOr",)], dma=True)
            pe_settle([("WOr",)])
            for tc in range(TC):
                bank = PS[tc % 4]
                ops = []
                for dinc in range(16):
                    ops.append((mm(bank[:, :],
                                   AO[:, dinc * 2048 + tc * 128: dinc * 2048 + (tc + 1) * 128],
                                   WOr[:, dinc * 512:(dinc + 1) * 512],
                                   dinc == 0, dinc == 15), 0))
                ops[-1] = (ops[-1][0], 1)
                p.g("tensor", ops, reads=ao_reads + [("WOr",)], writes=[("ps", tc % 4)])
                sl = tc % 2
                p.g("scalar", [
                    (lambda e, sl=sl, bank=bank: e.activation(
                        ytile[sl][:, :], bank[:, :], AF.Copy, bias=0.0, scale=1.0), 1),
                ], reads=[("ps", tc % 4)], writes=[("ytile", sl)])
                p.g("sync", [
                    (lambda e, sl=sl, tc=tc, q=q: e.dma_start(
                        out=ys[tc * 128:(tc + 1) * 128, q * 512:(q + 1) * 512],
                        in_=ytile[sl][:, :]), 16),
                ], reads=[("ytile", sl)], writes=[("ys", tc, q)], dma=True)
        for tc in range(TC):
            sl = tc % 2
            yf = yfull[0]
            xc = xch[0]
            x2 = x2t[sl]
            p.g("sync", [
                (lambda e, yf=yf, tc=tc: e.dma_start(out=yf[:, :], in_=ys[tc * 128:(tc + 1) * 128, :]), 16),
                (lambda e, xc=xc, tc=tc, seq=seq: e.dma_start(
                    out=xc[:, :], in_=x_in[seq, tc * 128:(tc + 1) * 128, :]), 16),
            ], reads=[("ys", tc, 0), ("ys", tc, 1), ("ys", tc, 2), ("ys", tc, 3)],
                writes=[("yfull", 0), ("xch", 0)], dma=True)
            c0 = 4 * sl
            plan_rowln(yf, ("yfull", 0), tmpf, ("tmpf",), c0, ("c", sl))
            p.g("vector", [
                (lambda e, x2=x2, xc=xc, c0=c0: e.scalar_tensor_tensor(
                    x2[:, :], tmpf[:, :], stats[:, c0 + 3:c0 + 4], xc[:, :],
                    ALU.mult, ALU.add), 1),
            ], reads=[("tmpf",), ("xch", 0), ("stats", ("c", sl))], writes=[("x2t", sl)])
            p.g("sync", [
                (lambda e, x2=x2, tc=tc: e.dma_start(out=x2s[tc * 128:(tc + 1) * 128, :], in_=x2[:, :]), 16),
            ], reads=[("x2t", sl)], writes=[("x2s", tc)], dma=True)
            plan_rowln(x2, ("x2t", sl), tmpf, ("tmpf",), c0, ("c", sl))
            p.g("vector", [
                (lambda e, yf=yf, c0=c0: e.tensor_scalar(
                    yf[:, :], tmpf[:, :], stats[:, c0 + 3:c0 + 4], 1.0,
                    ALU.mult, ALU.mult), 1),
            ], reads=[("tmpf",), ("stats", ("c", sl))], writes=[("yfull", 0)])
            for g4 in range(4):
                bank = PS[4 + g4 % 2]
                tr_ops = []
                for j in range(4):
                    dinc = g4 * 4 + j
                    tr_ops.append((lambda e, yf=yf, dinc=dinc, j=j, bank=bank: e.transpose(
                        bank[:, j * 128:(j + 1) * 128],
                        yf[:, dinc * 128:(dinc + 1) * 128], ident[:, :]), 0))
                tr_ops[-1] = (tr_ops[-1][0], 1)
                p.g("tensor", tr_ops, reads=[("yfull", 0), ("ident",)], writes=[("ps", 4 + g4 % 2)])
                xnap = XN[:, :].rearrange("p (dinc t) -> p dinc t", dinc=16)
                p.g("scalar", [
                    (lambda e, g4=g4, tc=tc, bank=bank, xnap=xnap: e.activation(
                        xnap[:, g4 * 4:g4 * 4 + 4, tc * 128:(tc + 1) * 128],
                        bank[:, :], AF.Copy, bias=0.0, scale=1.0), 1),
                ], reads=[("ps", 4 + g4 % 2)], writes=[("XN", tc)])
        if DBG and seq == 0:
            p.g("sync", [(lambda e: e.dma_start(out=dxn2, in_=XN[:, :]), 16)],
                reads=[("XN", t) for t in range(TC)], dma=True)
        close_scope(c_sc)
        close_scope(attn_sc)
        p.barrier()

        ffn_sc = mk_scope()
        WG = [sb(ffn_sc, f"WG{seq}_{i}", [128, 2048], BF16) for i in range(2)]
        WU = [sb(ffn_sc, f"WU{seq}_{i}", [128, 2048], BF16) for i in range(2)]
        GS = [sb(ffn_sc, f"GS{seq}_{i}", [128, 512], BF16) for i in range(2)]
        HST = [sb(ffn_sc, f"HST{seq}_{i}", [128, 512], BF16) for i in range(2)]
        HBLK = sb(ffn_sc, f"HBLK{seq}", [128, ICC * 512], BF16)
        WDc = [sb(ffn_sc, f"WDc{seq}_{i}", [128, 1024], BF16) for i in range(2)]
        ybl = [sb(ffn_sc, f"ybl{seq}_{i}", [128, 2048], F32) for i in range(4)]
        x2e = [sb(ffn_sc, f"x2e{seq}_0", [128, 2048], F32)]
        tmpe = sb(ffn_sc, f"tmpe{seq}", [128, 2048], F32)

        for icc in range(ICC):
            sl = icc % 2
            p.g("sync", [
                (lambda e, sl=sl, icc=icc: e.dma_start(
                    out=WG[sl][:, :], in_=wall_ap[icc * 128:(icc + 1) * 128, :]), 16),
                (lambda e, sl=sl, icc=icc: e.dma_start(
                    out=WU[sl][:, :], in_=wall_ap[(ICC + icc) * 128:(ICC + icc + 1) * 128, :]), 16),
            ], reads=[("wall",)], writes=[("WG", sl), ("WU", sl)], dma=True)
            if icc < 2:
                pe_settle([("WG", sl), ("WU", sl)])
            for tb in range(4):
                bgi = (2 * tb) % 4
                bui = (2 * tb + 1) % 4
                bg = PS[bgi]
                bu = PS[bui]
                for (w, wkey, bank, bid) in ((WG[sl], "WG", bg, bgi), (WU[sl], "WU", bu, bui)):
                    ops = []
                    for dinc in range(16):
                        ops.append((mm(bank[:, :],
                                       w[:, dinc * 128:(dinc + 1) * 128],
                                       XN[:, dinc * 2048 + tb * 512: dinc * 2048 + (tb + 1) * 512],
                                       dinc == 0, dinc == 15), 0))
                    ops[-1] = (ops[-1][0], 1)
                    p.g("tensor", ops, reads=xn_reads + [(wkey, sl)], writes=[("ps", bid)])
                ssl = tb % 2
                p.g("scalar", [
                    (lambda e, ssl=ssl, bg=bg: e.activation(GS[ssl][:, :], bg[:, :], AF.Silu,
                                                            bias=0.0, scale=1.0), 1),
                ], reads=[("ps", bgi)], writes=[("GS", ssl)])
                p.g("vector", [
                    (lambda e, ssl=ssl, bu=bu: e.scalar_tensor_tensor(
                        HST[ssl][:, :], bu[:, :], 1.0, GS[ssl][:, :], ALU.mult, ALU.mult), 1),
                ], reads=[("ps", bui), ("GS", ssl)], writes=[("HST", ssl)])
                p.g("sync", [
                    (lambda e, ssl=ssl, tb=tb, icc=icc: e.dma_start(
                        out=hs[tb * 128:(tb + 1) * 128, icc * 512:(icc + 1) * 512],
                        in_=HST[ssl][:, :]), 16),
                ], reads=[("HST", ssl)], writes=[("hs", icc, tb)], dma=True)
        p.barrier()

        for tb in range(4):
            hs_reads = [("hs", icc, tb) for icc in range(ICC)]
            p.g("sync", [
                (lambda e, tb=tb: e.dma_start(out=HBLK[:, :],
                                              in_=hs[tb * 128:(tb + 1) * 128, :]), 16),
            ], reads=hs_reads, writes=[("HBLK",)], dma=True)
            pe_settle([("HBLK",)], n=24)
            for half in range(2):
                for qp in range(2):
                    for icc in range(ICC):
                        sl = icc % 2
                        dblk = 2 * ICC + icc
                        p.g("sync", [
                            (lambda e, sl=sl, dblk=dblk, half=half: e.dma_start(
                                out=WDc[sl][:, :],
                                in_=wall_ap[dblk * 128:(dblk + 1) * 128,
                                            half * 1024:(half + 1) * 1024]), 16),
                        ], reads=[("wall",)], writes=[("WDc", sl)], dma=True)
                        if icc < 2:
                            pe_settle([("WDc", sl)])
                        ops_by_bank = []
                        for ti in range(2):
                            tcc = 2 * qp + ti
                            for nb in range(2):
                                bid = ti * 2 + nb
                                ops_by_bank.append((mm(
                                    PS[bid][:, :],
                                    HBLK[:, icc * 512 + tcc * 128: icc * 512 + (tcc + 1) * 128],
                                    WDc[sl][:, nb * 512:(nb + 1) * 512],
                                    icc == 0, icc == ICC - 1), 1 if bid == 3 else 0))
                        p.g("tensor", ops_by_bank,
                            reads=[("HBLK",), ("WDc", sl)],
                            writes=[("ps", i) for i in range(4)])
                    for ti in range(2):
                        tcc = 2 * qp + ti
                        for nb in range(2):
                            bid = ti * 2 + nb
                            p.g("scalar", [
                                (lambda e, tcc=tcc, nb=nb, half=half, bid=bid: e.activation(
                                    ybl[tcc][:, half * 1024 + nb * 512: half * 1024 + (nb + 1) * 512],
                                    PS[bid][:, :], AF.Copy, bias=0.0, scale=1.0), 1),
                            ], reads=[("ps", bid)], writes=[("ybl", tcc)])
            if DBG and seq == 1:
                p.g("sync", [(lambda e, tb=tb: e.dma_start(out=dybl[tb], in_=ybl[0][:, :]), 16)],
                    reads=[("ybl", 0)], dma=True)
            for tcc in range(4):
                tc = tb * 4 + tcc
                sl = tcc % 2
                x2 = x2e[0]
                p.g("sync", [
                    (lambda e, x2=x2, tc=tc: e.dma_start(
                        out=x2[:, :], in_=x2s[tc * 128:(tc + 1) * 128, :]), 16),
                ], reads=[("x2s", tc)], writes=[("x2e", 0)], dma=True)
                if DBG and seq == 1 and tcc == 0:
                    p.g("sync", [(lambda e, tb=tb, x2=x2: e.dma_start(out=dx2e[tb], in_=x2[:, :]), 16)],
                        reads=[("x2e", 0)], dma=True)
                c0 = 4 * sl
                plan_rowln(ybl[tcc], ("ybl", tcc), tmpe, ("tmpe",), c0, ("e", sl))
                p.g("vector", [
                    (lambda e, tcc=tcc, x2=x2, c0=c0: e.scalar_tensor_tensor(
                        ybl[tcc][:, :], tmpe[:, :], stats[:, c0 + 3:c0 + 4], x2[:, :],
                        ALU.mult, ALU.add), 1),
                ], reads=[("tmpe",), ("x2e", 0), ("stats", ("e", sl))], writes=[("ybl", tcc)])
                p.g("sync", [
                    (lambda e, tcc=tcc, tc=tc, seq=seq: e.dma_start(
                        out=y_out[seq, tc * 128:(tc + 1) * 128, :], in_=ybl[tcc][:, :]), 16),
                ], reads=[("ybl", tcc)], writes=[("yout", seq, tc)], dma=True)
        close_scope(ffn_sc)
        p.barrier()

    final_sync = p.cnt["syncq"]
    p.steps["sync"].append(("wait", "syncq", final_sync))

    semh = {}
    for name in ("pe", "dve", "act", "pool", "syncq", "cc"):
        gg = nc.semaphore(name)
        semh[name] = gg.__enter__()
        all_ctx.append(gg)

    blk = nc.Block()
    block = blk.__enter__()

    @block.tensor
    def _(eng):
        p.emit("tensor", eng, semh)

    @block.vector
    def _(eng):
        p.emit("vector", eng, semh)

    @block.scalar
    def _(eng):
        p.emit("scalar", eng, semh)

    @block.gpsimd
    def _(eng):
        p.emit("gpsimd", eng, semh)

    @block.sync
    def _(eng):
        p.emit("sync", eng, semh)

    blk.__exit__(None, None, None)
    for gg in reversed(all_ctx):
        gg.__exit__(None, None, None)
    close_scope(persist)
    return nc


def _prep_inputs(L0_attn_x, L0_attn_wq, L0_attn_wk, L0_attn_wv, L0_attn_wo,
                 L0_ffn_lut, L0_ffn_gate_idx, L0_ffn_up_idx, L0_ffn_down_idx):
    x = np.ascontiguousarray(np.asarray(L0_attn_x, dtype=np.float32))
    wq = np.asarray(L0_attn_wq, dtype=np.float32)
    wk = np.asarray(L0_attn_wk, dtype=np.float32)
    wv = np.asarray(L0_attn_wv, dtype=np.float32)
    wo = np.asarray(L0_attn_wo, dtype=np.float32)
    lut = np.ascontiguousarray(np.asarray(L0_ffn_lut, dtype=np.float32))
    gidx = np.asarray(L0_ffn_gate_idx, dtype=np.int32)
    uidx = np.asarray(L0_ffn_up_idx, dtype=np.int32)
    didx = np.asarray(L0_ffn_down_idx, dtype=np.int32)

    wq_t = np.ascontiguousarray(wq.reshape(16, 128, 16, 128).transpose(0, 3, 2, 1))
    wk_t = np.ascontiguousarray(wk.reshape(16, 128, 16, 128).transpose(0, 3, 2, 1))
    wv_t = np.ascontiguousarray(wv.reshape(8, 256, 16, 128).transpose(0, 3, 2, 1))
    wo_t = np.ascontiguousarray(wo.reshape(4, 512, 16, 128).transpose(0, 3, 2, 1))

    wg_wall = gidx.T.reshape(16, 128, ICC, 128).transpose(2, 1, 0, 3).reshape(ICC * 128, 2048)
    wu_wall = uidx.T.reshape(16, 128, ICC, 128).transpose(2, 1, 0, 3).reshape(ICC * 128, 2048)
    wd_wall = didx.T.reshape(ICC, 128, 16, 128).reshape(ICC * 128, 2048)
    wallidx = np.ascontiguousarray(np.concatenate([wg_wall, wu_wall, wd_wall], axis=0))

    ident = np.eye(128, dtype=np.float32)
    in_maps = []
    for c in range(NCORES):
        in_maps.append({
            "x": np.ascontiguousarray(x[c * SEQ_PER_CORE:(c + 1) * SEQ_PER_CORE]),
            "wq_t": wq_t, "wk_t": wk_t, "wv_t": wv_t, "wo_t": wo_t,
            "lut": lut,
            "idx_shard": np.ascontiguousarray(wallidx[c * SHARD_ROWS:(c + 1) * SHARD_ROWS]),
            "ident": ident,
        })
    return in_maps


def kernel(L0_attn_x, L0_attn_wq, L0_attn_wk, L0_attn_wv, L0_attn_wo,
           L0_ffn_lut, L0_ffn_gate_idx, L0_ffn_up_idx, L0_ffn_down_idx):
    in_maps = _prep_inputs(L0_attn_x, L0_attn_wq, L0_attn_wk, L0_attn_wv, L0_attn_wo,
                           L0_ffn_lut, L0_ffn_gate_idx, L0_ffn_up_idx, L0_ffn_down_idx)
    nc = build_nc()
    res = run_bass_kernel_spmd(nc, in_maps, core_ids=list(range(NCORES)))
    out = np.concatenate([res.results[c]["y"] for c in range(NCORES)], axis=0)
    return out


# version 13 (final)
# speedup vs baseline: 1.0095x; 1.0095x over previous
import numpy as np
import ml_dtypes
import concourse.bass as bass
import concourse.mybir as mybir
from concourse.bass_utils import run_bass_kernel_spmd

F32 = mybir.dt.float32
BF16 = mybir.dt.bfloat16
I32 = mybir.dt.int32
AF = mybir.ActivationFunctionType
ALU = mybir.AluOpType
AX = mybir.AxisListType

NCORES = 8
B, S, D = 16, 2048, 2048
NH, HD = 16, 128
INTER, ICC = 5504, 43
SEQ_PER_CORE = B // NCORES
TC = S // 128
SCALE = 1.0 / float(np.sqrt(HD))
EPS = 1e-5
WALL_ROWS = 3 * ICC * 128
SHARD_ROWS = WALL_ROWS // NCORES
DEC_UNITS = ((SHARD_ROWS + 127) // 128) * 4

DBG = False
LIMIT = ""

ENGINES = ("tensor", "vector", "scalar", "gpsimd", "sync")
ESEM = {"tensor": "pe", "vector": "dve", "scalar": "act", "gpsimd": "pool", "sync": "syncq"}


class Plan:
    def __init__(self):
        self.steps = {e: [] for e in ENGINES}
        self.cnt = {"pe": 0, "dve": 0, "act": 0, "pool": 0, "syncq": 0, "cc": 0}
        self.bufs = {}
        self.water = {e: {} for e in ENGINES}
        self.dummy_fns = None

    def _buf(self, k):
        if k not in self.bufs:
            self.bufs[k] = {"w": None, "r": {}}
        return self.bufs[k]

    def _need(self, engine, dep):
        s, c, is_async = dep
        if s == ESEM[engine] and not is_async:
            return
        if s == "pe" and engine != "tensor":
            if self.cnt["pe"] >= c + 2:
                c = c + 2
            else:
                self.cnt["pe"] += 1
                self.steps["tensor"].append(("ops", [
                    (self.dummy_fns[0], 0),
                    (self.dummy_fns[1], 1),
                ], "pe"))
                c = self.cnt["pe"]
        if self.water[engine].get(s, -1) >= c:
            return
        self.steps[engine].append(("wait", s, c))
        self.water[engine][s] = c

    def g(self, engine, ops, reads=(), writes=(), sem=None, dma=False):
        if dma:
            hot = False
            for k in reads:
                b = self._buf(k)
                if b["w"] and b["w"][0] in ("dve", "act", "pool"):
                    hot = True
            if hot:
                ops = [(lambda e: e.nop(cycle_cnt=250, nofuse=True), 0)] * 4 + list(ops)
        for k in reads:
            b = self._buf(k)
            if b["w"]:
                self._need(engine, b["w"])
        for k in writes:
            b = self._buf(k)
            if b["w"]:
                self._need(engine, b["w"])
            for s, (c, ia) in list(b["r"].items()):
                self._need(engine, (s, c, ia))
        semn = sem or ESEM[engine]
        tot = sum(i for _, i in ops)
        self.cnt[semn] += tot
        is_async = dma or semn == "cc"
        mile = (semn, self.cnt[semn], is_async)
        self.steps[engine].append(("ops", ops, semn))
        for k in writes:
            b = self._buf(k)
            b["w"] = mile
            b["r"] = {}
        for k in reads:
            b = self._buf(k)
            cur = b["r"].get(semn, (-1, False))
            if mile[1] > cur[0]:
                b["r"][semn] = (mile[1], is_async)
        return mile

    def barrier(self):
        snap = dict(self.cnt)
        for e in ENGINES:
            for s in ("pe", "dve", "act", "pool", "syncq", "cc"):
                if snap[s] > 0:
                    self._need(e, (s, snap[s], True))

    def emit(self, engine, eng, semh):
        for st in self.steps[engine]:
            if st[0] == "wait":
                eng.wait_ge(semh[st[1]], st[2])
            else:
                _, ops, semn = st
                for fn, inc in ops:
                    inst = fn(eng)
                    if inc:
                        inst.then_inc(semh[semn], inc)


def build_nc():
    nc = bass.Bass("TRN2", target_bir_lowering=False)
    p = Plan()

    x_in = nc.dram_tensor("x", [SEQ_PER_CORE, S, D], F32, kind="ExternalInput").ap()
    y_out = nc.dram_tensor("y", [SEQ_PER_CORE, S, D], F32, kind="ExternalOutput").ap()
    wq_d = nc.dram_tensor("wq_t", [NH, 128, 16, 128], F32, kind="ExternalInput").ap()
    wk_d = nc.dram_tensor("wk_t", [NH, 128, 16, 128], F32, kind="ExternalInput").ap()
    wv_d = nc.dram_tensor("wv_t", [NH // 2, 128, 16, 256], F32, kind="ExternalInput").ap()
    wo_d = nc.dram_tensor("wo_t", [4, 128, 16, 512], F32, kind="ExternalInput").ap()
    lut_d = nc.dram_tensor("lut", [16], F32, kind="ExternalInput").ap()
    idx_d = nc.dram_tensor("idx_shard", [SHARD_ROWS, 2048], I32, kind="ExternalInput").ap()
    ident_d = nc.dram_tensor("ident", [128, 128], F32, kind="ExternalInput").ap()

    shard = nc.dram_tensor("shard", [SHARD_ROWS, 2048], BF16, kind="Internal")
    wall = nc.dram_tensor("wall", [WALL_ROWS, 2048], BF16, kind="Internal")
    _sk = "ExternalOutput" if DBG else "Internal"
    ys = nc.dram_tensor("ys", [S, D], F32, kind=_sk).ap()
    x2s = nc.dram_tensor("x2s", [S, D], F32, kind=_sk).ap()
    hs = nc.dram_tensor("hs", [4 * 128, ICC * 512], BF16, kind=_sk).ap()
    wall_ap = wall.ap()
    if DBG:
        dxn1 = nc.dram_tensor("dxn1", [128, 16 * 2048], BF16, kind="ExternalOutput").ap()
        dxn2 = nc.dram_tensor("dxn2", [128, 16 * 2048], BF16, kind="ExternalOutput").ap()
        dq = nc.dram_tensor("dq", [128, 2048], BF16, kind="ExternalOutput").ap()
        dk = nc.dram_tensor("dk", [128, 2048], BF16, kind="ExternalOutput").ap()
        dv4 = nc.dram_tensor("dv4", [128, 4096], BF16, kind="ExternalOutput").ap()
        dtiny = nc.dram_tensor("dtiny", [128, 16], F32, kind="ExternalOutput").ap()
        dao = nc.dram_tensor("dao", [128, 16 * 2048], BF16, kind="ExternalOutput").ap()
        dwall = nc.dram_tensor("dwall", [256, 2048], BF16, kind="ExternalOutput").ap()
        dcs = nc.dram_tensor("dcs", [128, 64], F32, kind="ExternalOutput").ap()
        dybl = nc.dram_tensor("dybl", [4, 128, 2048], F32, kind="ExternalOutput").ap()
        dx2e = nc.dram_tensor("dx2e", [4, 128, 2048], F32, kind="ExternalOutput").ap()

    all_ctx = []

    def mk_scope():
        return []

    def sb(scope, name, shape, dt):
        gg = nc.sbuf_tensor(name, shape, dt)
        t = gg.__enter__()
        scope.append(gg)
        return t

    def close_scope(scope):
        for gg in reversed(scope):
            gg.__exit__(None, None, None)

    def ps_alloc(scope, name):
        gg = nc.psum_tensor(name, [128, 512], F32)
        t = gg.__enter__()
        scope.append(gg)
        return t

    persist = mk_scope()
    XN = sb(persist, "XN", [128, 16 * 2048], BF16)
    ident = sb(persist, "ident_s", [128, 128], F32)
    ones_f = sb(persist, "ones_f", [128, 1], F32)
    ones_bf = sb(persist, "ones_bf", [128, 1], BF16)
    ones_row = sb(persist, "ones_row", [1, 128], F32)
    lut_s = sb(persist, "lut_sb", [128, 16], F32)
    delta_s = sb(persist, "delta_sb", [128, 16], F32)
    stats = sb(persist, "stats", [128, 32], F32)
    cstat = sb(persist, "cstat", [128, 64], F32)
    csq = sb(persist, "csq", [128, 64], F32)
    tiny = sb(persist, "tiny", [128, 16], F32)
    row_s = sb(persist, "row_s", [1, 520], F32)
    bc_sb = sb(persist, "bc_sb", [128, 512], F32)
    trash = sb(persist, "trash", [128, 2048], BF16)
    xch = [sb(persist, "xch0", [128, 2048], F32)]
    dum_rhs = sb(persist, "dum_rhs", [128, 256], BF16)
    PS = [ps_alloc(persist, f"ps{i}") for i in range(8)]
    p.dummy_fns = (
        lambda e: e.matmul(PS[7][0:1, 0:256], ones_bf[:, :], dum_rhs[:, :],
                           start=True, stop=True),
        lambda e: e.matmul(PS[7][0:1, 256:512], ones_bf[:, :], dum_rhs[:, :],
                           start=True, stop=True),
    )

    def pe_settle(keys, n=16):
        ops = []
        for i in range(n):
            ops.append((p.dummy_fns[i % 2], 0))
        ops[-1] = (ops[-1][0], 1)
        p.g("tensor", ops, reads=list(keys), writes=[("ps", 7)])

    def mm(out_ap, lhs_ap, rhs_ap, start, stop):
        return lambda e: e.matmul(out_ap, lhs_ap, rhs_ap, start=start, stop=stop)

    CONSTS = [0.0, 1.0, -1.0, 0.5, -0.5, EPS, 1e-20, SCALE, -SCALE, 1.0 / D]
    cvals = sb(persist, "cvals", [128, len(CONSTS)], F32)
    for i, v in enumerate(CONSTS):
        nc.const_aps.register(F32, v, cvals[:, i:i + 1])

    init_ops = [
        (lambda e: e.memset(dum_rhs[:, :], 0.0), 0),
        (lambda e: e.memset(ones_f[:, :], 1.0), 0),
        (lambda e: e.memset(ones_bf[:, :], 1.0), 0),
        (lambda e: e.memset(ones_row[:, :], 1.0), 0),
    ]
    for i, v in enumerate(CONSTS):
        init_ops.append((lambda e, i=i, v=v: e.memset(cvals[:, i:i + 1], v), 0))
    init_ops[-1] = (init_ops[-1][0], 1)
    p.g("vector", init_ops, writes=[("ones",)])
    p.g("sync", [
        (lambda e: e.dma_start(out=ident[:, :], in_=ident_d), 16),
        (lambda e: e.dma_start(out=lut_s[:, :], in_=lut_d.unsqueeze(0).partition_broadcast(128)), 16),
    ], writes=[("ident",), ("lut",)], dma=True)
    p.g("vector", [
        (lambda e: e.tensor_tensor(delta_s[:, 1:16], lut_s[:, 1:16], lut_s[:, 0:15], ALU.subtract), 1),
    ], reads=[("lut",)], writes=[("delta",)])
    p.barrier()

    def plan_rowln(src_t, src_key, work_t, work_key, c0, slotkey):
        p.g("scalar", [
            (lambda e: e.activation(trash[:, 0:src_t.shape[1]], src_t[:, :], AF.Copy,
                                    bias=0.0, scale=1.0,
                                    accum_out=stats[:, c0:c0 + 1]), 1),
        ], reads=[src_key], writes=[("stats", slotkey), ("trash",), ("trashrow", 0),
                                    ("trashrow", 1), ("trashrow", 2), ("trashrow", 3)])
        p.g("gpsimd", [
            (lambda e: e.tensor_scalar(stats[:, c0:c0 + 1], stats[:, c0:c0 + 1],
                                       1.0 / D, None, ALU.mult), 1),
        ], reads=[("stats", slotkey)], writes=[("stats", slotkey)])
        p.g("vector", [
            (lambda e: e.tensor_scalar(work_t[:, :], src_t[:, :], stats[:, c0:c0 + 1],
                                       1.0, ALU.subtract, ALU.mult), 1),
        ], reads=[src_key, ("stats", slotkey)], writes=[work_key])
        p.g("scalar", [
            (lambda e: e.activation(trash[:, 0:work_t.shape[1]], work_t[:, :], AF.Square,
                                    bias=0.0, scale=1.0,
                                    accum_out=stats[:, c0 + 1:c0 + 2]), 0),
            (lambda e: e.activation(stats[:, c0 + 2:c0 + 3], stats[:, c0 + 1:c0 + 2],
                                    AF.Ln, bias=EPS, scale=1.0 / D), 1),
        ], reads=[work_key, ("stats", slotkey)], writes=[("stats", slotkey), ("trash",),
                                    ("trashrow", 0), ("trashrow", 1), ("trashrow", 2), ("trashrow", 3)])
        p.g("gpsimd", [
            (lambda e: e.tensor_scalar(stats[:, c0 + 3:c0 + 4], stats[:, c0 + 2:c0 + 3],
                                       -0.5, None, ALU.mult), 1),
        ], reads=[("stats", slotkey)], writes=[("stats", slotkey)])
        p.g("scalar", [
            (lambda e: e.activation(stats[:, c0 + 3:c0 + 4], stats[:, c0 + 3:c0 + 4],
                                    AF.Exp, bias=0.0, scale=1.0), 1),
        ], reads=[("stats", slotkey)], writes=[("stats", slotkey)])

    def plan_ln_to_xn(seq, xnt):
        for tc in range(TC):
            slot = tc % 2
            xc = xch[0]
            src = x_in[seq, tc * 128:(tc + 1) * 128, :]
            p.g("sync", [(lambda e, xc=xc, src=src: e.dma_start(out=xc[:, :], in_=src), 16)],
                writes=[("xch", 0)], dma=True)
            c0 = 4 * slot
            plan_rowln(xc, ("xch", 0), xnt, ("xnt",), c0, slot)
            p.g("vector", [
                (lambda e, xc=xc, c0=c0: e.tensor_scalar(
                    xc[:, :], xnt[:, :], stats[:, c0 + 3:c0 + 4], 1.0,
                    ALU.mult, ALU.mult), 1),
            ], reads=[("xnt",), ("stats", slot)], writes=[("xch", 0)])
            for g4 in range(4):
                bank = PS[g4 % 2]
                tr_ops = []
                for j in range(4):
                    dinc = g4 * 4 + j
                    tr_ops.append((lambda e, xc=xc, dinc=dinc, j=j, bank=bank: e.transpose(
                        bank[:, j * 128:(j + 1) * 128],
                        xc[:, dinc * 128:(dinc + 1) * 128], ident[:, :]), 0))
                tr_ops[-1] = (tr_ops[-1][0], 1)
                p.g("tensor", tr_ops, reads=[("xch", 0), ("ident",)], writes=[("ps", g4 % 2)])
                xnap = XN[:, :].rearrange("p (dinc t) -> p dinc t", dinc=16)
                p.g("scalar", [
                    (lambda e, g4=g4, tc=tc, bank=bank, xnap=xnap: e.activation(
                        xnap[:, g4 * 4:g4 * 4 + 4, tc * 128:(tc + 1) * 128],
                        bank[:, :], AF.Copy, bias=0.0, scale=1.0), 1),
                ], reads=[("ps", g4 % 2)], writes=[("XN", tc)])

    for seq in range(SEQ_PER_CORE):
        attn_sc = mk_scope()
        AO = sb(attn_sc, f"AO{seq}", [128, 16 * 2048], BF16)
        ab_sc = mk_scope()
        xnt = sb(ab_sc, f"xnt{seq}", [128, 2048], F32)
        WQh = sb(ab_sc, f"WQh{seq}", [128, 2048], BF16)
        WKh = sb(ab_sc, f"WKh{seq}", [128, 2048], BF16)
        WVg = sb(ab_sc, f"WVg{seq}", [128, 16 * 256], BF16)
        Qh = sb(ab_sc, f"Qh{seq}", [128, 2048], BF16)
        Kh = sb(ab_sc, f"Kh{seq}", [128, 2048], BF16)
        Vg = sb(ab_sc, f"Vg{seq}", [128, 16 * 256], BF16)
        PT = [sb(ab_sc, f"PT{seq}_{i}", [128, 512], BF16) for i in range(2)]
        PTM = [sb(ab_sc, f"PTM{seq}_{i}", [128, 512], BF16) for i in range(2)]
        S_acc = sb(ab_sc, f"S_acc{seq}", [128, 512], F32)
        

        if seq == 0:
            dec_sc = mk_scope()
            IDXT = [sb(dec_sc, f"IDXT{i}", [128, 512], BF16) for i in range(2)]
            DACC = sb(dec_sc, "DACC", [128, 512], F32)
            DACP = sb(dec_sc, "DACP", [128, 512], F32)
            DTRM = [sb(dec_sc, f"DTRM{i}", [128, 512], F32) for i in range(2)]
            WOUT = [sb(dec_sc, f"WOUT{i}", [128, 512], BF16) for i in range(2)]
            dec_state = {"i": 0}

            def plan_decode_unit():
                u = dec_state["i"]
                if u >= DEC_UNITS:
                    return
                dec_state["i"] += 1
                rc, cq = divmod(u, 4)
                r0 = rc * 128
                rows = min(128, SHARD_ROWS - r0)
                c0_ = cq * 512
                slot = u % 2
                it = IDXT[slot]
                wo_ = WOUT[slot]
                p.g("gpsimd", [
                    (lambda e, it=it, r0=r0, rows=rows, c0_=c0_: e.dma_start(
                        out=it[0:rows, :], in_=idx_d[r0:r0 + rows, c0_:c0_ + 512]), 16),
                ], writes=[("IDXT", slot)], dma=True)
                dve_ops = []
                for v in range(1, 9):
                    ts = DTRM[0]
                    if v == 1:
                        dve_ops.append((lambda e, v=v, it=it, rows=rows: e.tensor_scalar(
                            DACC[0:rows, :], it[0:rows, :], float(v) - 0.5,
                            delta_s[0:rows, v:v + 1], ALU.is_ge, ALU.mult), 0))
                    else:
                        dve_ops.append((lambda e, v=v, it=it, ts=ts, rows=rows: e.tensor_scalar(
                            ts[0:rows, :], it[0:rows, :], float(v) - 0.5,
                            delta_s[0:rows, v:v + 1], ALU.is_ge, ALU.mult), 0))
                        dve_ops.append((lambda e, ts=ts, rows=rows: e.tensor_tensor(
                            DACC[0:rows, :], DACC[0:rows, :], ts[0:rows, :], ALU.add), 0))
                dve_ops[-1] = (dve_ops[-1][0], 1)
                p.g("vector", dve_ops, reads=[("IDXT", slot), ("delta",)],
                    writes=[("DACC",), ("DTRM", 0)])
                for v in range(9, 16):
                    ts = DTRM[v % 2]
                    p.g("vector", [
                        (lambda e, v=v, it=it, ts=ts, rows=rows: e.tensor_scalar(
                            ts[0:rows, :], it[0:rows, :], float(v) - 0.5,
                            delta_s[0:rows, v:v + 1], ALU.is_ge, ALU.mult), 1),
                    ], reads=[("IDXT", slot), ("delta",)], writes=[("DTRM", v % 2)])
                    if v == 9:
                        p.g("gpsimd", [
                            (lambda e, ts=ts, rows=rows: e.tensor_copy(
                                DACP[0:rows, :], ts[0:rows, :]), 1),
                        ], reads=[("DTRM", v % 2)], writes=[("DACP",)])
                    else:
                        p.g("gpsimd", [
                            (lambda e, ts=ts, rows=rows: e.tensor_tensor(
                                DACP[0:rows, :], DACP[0:rows, :], ts[0:rows, :], ALU.add), 1),
                        ], reads=[("DTRM", v % 2)], writes=[("DACP",)])
                p.g("vector", [
                    (lambda e, wo_=wo_, rows=rows: e.scalar_tensor_tensor(
                        wo_[0:rows, :], DACC[0:rows, :], lut_s[0:rows, 0:1], DACP[0:rows, :],
                        ALU.add, ALU.add), 1),
                ], reads=[("DACC",), ("DACP",), ("lut",)], writes=[("WOUT", slot)])
                p.g("sync", [
                    (lambda e, wo_=wo_, r0=r0, rows=rows, c0_=c0_: e.dma_start(
                        out=shard.ap()[r0:r0 + rows, c0_:c0_ + 512], in_=wo_[0:rows, :]), 16),
                ], reads=[("WOUT", slot)], writes=[("shard_part", u)], dma=True)
        else:
            def plan_decode_unit():
                pass

        def wload(h):
            if h >= NH:
                return
            p.g("gpsimd", [
                (lambda e, h=h: e.dma_start(out=WQh[:, :], in_=wq_d[h]), 16),
                (lambda e, h=h: e.dma_start(out=WKh[:, :], in_=wk_d[h]), 16),
            ], writes=[("WQh",), ("WKh",)], dma=True)
            if h % 2 == 0:
                g_ = h // 2
                p.g("gpsimd", [
                    (lambda e, g_=g_: e.dma_start(out=WVg[:, :], in_=wv_d[g_]), 16),
                ], writes=[("WVg",)], dma=True)

        xn_reads = [("XN", t) for t in range(TC)]

        def plan_A(h):
            if not _first_settle["done"]:
                _first_settle["done"] = True
                pe_settle([("WQh",), ("WKh",), ("WVg",)])
            if h % 2 == 0:
                for kc in range(16):
                    bank = PS[2 + kc % 2]
                    ops = []
                    for dinc in range(16):
                        ops.append((mm(bank[:, 0:256],
                                       XN[:, dinc * 2048 + kc * 128: dinc * 2048 + (kc + 1) * 128],
                                       WVg[:, dinc * 256:(dinc + 1) * 256],
                                       dinc == 0, dinc == 15), 0))
                    ops[-1] = (ops[-1][0], 1)
                    p.g("tensor", ops, reads=xn_reads + [("WVg",)], writes=[("ps", 2 + kc % 2)])
                    p.g("scalar", [
                        (lambda e, kc=kc, bank=bank: e.activation(
                            Vg[:, kc * 256:(kc + 1) * 256], bank[:, 0:256],
                            AF.Copy, bias=0.0, scale=1.0), 1),
                    ], reads=[("ps", 2 + kc % 2)], writes=[("Vg", kc)])
            for (wt, wkey, dst, key) in ((WQh, "WQh", Qh, "Qh"), (WKh, "WKh", Kh, "Kh")):
                for tb in range(4):
                    bank = PS[2 + tb % 2]
                    ops = []
                    for dinc in range(16):
                        ops.append((mm(bank[:, :],
                                       wt[:, dinc * 128:(dinc + 1) * 128],
                                       XN[:, dinc * 2048 + tb * 512: dinc * 2048 + (tb + 1) * 512],
                                       dinc == 0, dinc == 15), 0))
                    ops[-1] = (ops[-1][0], 1)
                    p.g("tensor", ops, reads=xn_reads + [(wkey,)], writes=[("ps", 2 + tb % 2)])
                    p.g("scalar", [
                        (lambda e, dst=dst, tb=tb, bank=bank: e.activation(
                            dst[:, tb * 512:(tb + 1) * 512], bank[:, :],
                            AF.Copy, bias=0.0, scale=1.0), 1),
                    ], reads=[("ps", 2 + tb % 2)], writes=[(key, tb)])
            for (src, srck, col) in ((Kh, "Kh", 8), (Qh, "Qh", 9)):
                p.g("scalar", [
                    (lambda e, src=src: e.activation(trash[:, :], src[:, :], AF.Square,
                                                     bias=0.0, scale=1.0), 1),
                ], reads=[(srck, t) for t in range(4)],
                    writes=[("trash",), ("trashrow", 0), ("trashrow", 1),
                            ("trashrow", 2), ("trashrow", 3)])
                for tb in range(4):
                    p.g("tensor", [
                        (mm(PS[6][0:1, :], ones_bf[:, :], trash[:, tb * 512:(tb + 1) * 512],
                            True, True), 1),
                    ], reads=[("trash",), ("ones",)], writes=[("ps", 6)])
                    p.g("vector", [
                        (lambda e, tb=tb: e.tensor_copy(
                            trash[0:1, tb * 512:(tb + 1) * 512], PS[6][0:1, :]), 1),
                    ], reads=[("ps", 6)], writes=[("trashrow", tb)])
                p.g("vector", [
                    (lambda e, col=col: e.tensor_reduce(
                        tiny[0:1, col:col + 1], trash[0:1, :], AX.X, ALU.max), 1),
                ], reads=[("trashrow", t) for t in range(4)], writes=[("tiny", col)])
            p.g("scalar", [
                (lambda e: e.activation(tiny[0:1, 10:11], tiny[0:1, 8:9], AF.Ln,
                                        bias=1e-20, scale=1.0), 0),
                (lambda e: e.activation(tiny[0:1, 11:12], tiny[0:1, 9:10], AF.Ln,
                                        bias=1e-20, scale=1.0), 1),
            ], reads=[("tiny", 8), ("tiny", 9)], writes=[("tiny", 10), ("tiny", 11)])
            p.g("gpsimd", [
                (lambda e: e.tensor_tensor(tiny[0:1, 12:13], tiny[0:1, 10:11],
                                           tiny[0:1, 11:12], ALU.add), 1),
            ], reads=[("tiny", 10), ("tiny", 11)], writes=[("tiny", 12)])
            p.g("vector", [
                (lambda e: e.tensor_scalar(tiny[0:1, 13:14], tiny[0:1, 12:13],
                                           0.5, None, ALU.mult), 1),
            ], reads=[("tiny", 12)], writes=[("tiny", 13)])
            p.g("scalar", [
                (lambda e: e.activation(tiny[0:1, 14:15], tiny[0:1, 13:14], AF.Exp,
                                        bias=0.0, scale=1.0), 1),
            ], reads=[("tiny", 13)], writes=[("tiny", 14)])
            p.g("gpsimd", [
                (lambda e: e.tensor_scalar(tiny[0:1, 12:13], tiny[0:1, 14:15], -SCALE, None,
                                           ALU.mult), 1),
            ], reads=[("tiny", 14)], writes=[("tiny", 12)])
            p.g("tensor", [
                (mm(PS[6][:, 0:1], ones_row[:, :], tiny[0:1, 12:13], True, True), 1),
            ], reads=[("tiny", 12), ("ones",)], writes=[("ps", 6)])
            p.g("scalar", [
                (lambda e: e.activation(tiny[:, 15:16], PS[6][:, 0:1],
                                        AF.Copy, bias=0.0, scale=1.0), 1),
            ], reads=[("ps", 6)], writes=[("negb",)])

        def plan_B(h):
            wload(h + 1)
            hh = h % 2
            for qb in range(4):
                kc_full = 4 * qb
                kc_max = min(4 * qb + 4, 15)
                for kc in range(kc_max + 1):
                    scb = kc % 2
                    p.g("tensor", [
                        (mm(PS[scb][:, :], Kh[:, kc * 128:(kc + 1) * 128],
                            Qh[:, qb * 512:(qb + 1) * 512], True, True), 1),
                    ], reads=[("Kh", kc // 4), ("Qh", qb)], writes=[("ps", scb)])
                    p.g("scalar", [
                        (lambda e, scb=scb: e.activation(
                            PT[scb][:, :], PS[scb][:, :], AF.Exp,
                            bias=tiny[:, 15:16], scale=SCALE), 1),
                    ], reads=[("ps", scb), ("negb",)], writes=[("PT", scb)])
                    src_pt = PT[scb]
                    ptkey = ("PT", scb)
                    if kc > kc_full:
                        base = qb * 512 - kc * 128 + 128
                        p.g("gpsimd", [
                            (lambda e, scb=scb, base=base: e.affine_select(
                                PTM[scb][:, :], PT[scb][:, :], [[1, 512]], ALU.is_ge,
                                0.0, base=base, channel_multiplier=-1), 1),
                        ], reads=[("PT", scb)], writes=[("PTM", scb)])
                        src_pt = PTM[scb]
                        ptkey = ("PTM", scb)
                    if kc == 0:
                        p.g("vector", [
                            (lambda e, src_pt=src_pt: e.tensor_copy(S_acc[:, :], src_pt[:, :]), 1),
                        ], reads=[ptkey], writes=[("S_acc",)])
                    else:
                        p.g("vector", [
                            (lambda e, src_pt=src_pt: e.tensor_tensor(
                                S_acc[:, :], S_acc[:, :], src_pt[:, :], ALU.add), 1),
                        ], reads=[ptkey], writes=[("S_acc",)])
                    p.g("tensor", [
                        (mm(PS[4][:, :],
                            Vg[:, kc * 256 + hh * 128: kc * 256 + hh * 128 + 128],
                            src_pt[:, :], kc == 0, kc == kc_max), 1),
                    ], reads=[("Vg", kc), ptkey], writes=[("ps", 4)])
                p.g("tensor", [
                    (mm(PS[5][0:1, :], ones_f[:, :], S_acc[:, :], True, True), 1),
                ], reads=[("S_acc",), ("ones",)], writes=[("ps", 5)])
                p.g("vector", [
                    (lambda e: e.reciprocal(row_s[0:1, 0:512], PS[5][0:1, :]), 1),
                ], reads=[("ps", 5)], writes=[("row0",)])
                p.g("tensor", [
                    (mm(PS[5][:, :], ones_row[:, :], row_s[0:1, 0:512], True, True), 1),
                ], reads=[("row0",), ("ones",)], writes=[("ps", 5)])
                p.g("scalar", [
                    (lambda e: e.activation(bc_sb[:, :], PS[5][:, :], AF.Copy,
                                            bias=0.0, scale=1.0), 1),
                ], reads=[("ps", 5)], writes=[("bc",)])
                p.g("vector", [
                    (lambda e, h=h, qb=qb: e.scalar_tensor_tensor(
                        AO[:, h * 2048 + qb * 512: h * 2048 + (qb + 1) * 512],
                        PS[4][:, :], 1.0, bc_sb[:, :], ALU.mult, ALU.mult), 1),
                ], reads=[("ps", 4), ("bc",)], writes=[("AO", h, qb)])

        plan_ln_to_xn(seq, xnt)
        _first_settle = {"done": False}
        if DBG and seq == 0:
            p.g("sync", [(lambda e: e.dma_start(out=dxn1, in_=XN[:, :]), 16)],
                reads=[("XN", t) for t in range(TC)], dma=True)
            p.g("sync", [(lambda e: e.dma_start(out=dcs, in_=cstat[:, :]), 16)],
                reads=[("cstat", t, 0) for t in range(TC)], dma=True)
        if LIMIT == "ln1":
            if seq == 0:
                close_scope(dec_sc)
            close_scope(ab_sc)
            close_scope(attn_sc)
            break
        wload(0)
        for h in range(NH):
            plan_A(h)
            if DBG and seq == 0 and h == 0:
                p.g("sync", [
                    (lambda e: e.dma_start(out=dq, in_=Qh[:, :]), 16),
                    (lambda e: e.dma_start(out=dk, in_=Kh[:, :]), 16),
                    (lambda e: e.dma_start(out=dv4, in_=Vg[:, :]), 16),
                    (lambda e: e.dma_start(out=dtiny, in_=tiny[:, :]), 16),
                ], reads=[("Qh", 0), ("Qh", 1), ("Qh", 2), ("Qh", 3),
                          ("Kh", 0), ("Kh", 1), ("Kh", 2), ("Kh", 3)] +
                         [("Vg", kc) for kc in range(16)] + [("negb",)], dma=True)
            plan_B(h)
            for _ in range(4):
                plan_decode_unit()
        if seq == 0:
            while dec_state["i"] < DEC_UNITS:
                plan_decode_unit()
            rds = [("shard_part", i) for i in range(DEC_UNITS)]
            p.g("gpsimd", [
                (lambda e: e.collective_compute(
                    "AllGather", ALU.bypass, [list(range(NCORES))],
                    [shard.ap().opt()], [wall_ap.opt()]), 1),
            ], reads=rds, writes=[("wall",)], sem="cc")
            close_scope(dec_sc)
        if DBG and seq == 0:
            p.g("sync", [(lambda e: e.dma_start(out=dao, in_=AO[:, :]), 16)],
                reads=[("AO", h, qb) for h in range(NH) for qb in range(4)], dma=True)
            p.g("sync", [(lambda e: e.dma_start(out=dwall, in_=wall_ap[2 * ICC * 128:2 * ICC * 128 + 256, :]), 16)],
                reads=[("wall",)], dma=True)
        close_scope(ab_sc)
        p.barrier()

        c_sc = mk_scope()
        WOr = sb(c_sc, f"WOr{seq}", [128, 16 * 512], BF16)
        ytile = [sb(c_sc, f"ytile{seq}_{i}", [128, 512], F32) for i in range(2)]
        yfull = [sb(c_sc, f"yfull{seq}_0", [128, 2048], F32)]
        x2t = [sb(c_sc, f"x2t{seq}_{i}", [128, 2048], F32) for i in range(2)]
        tmpf = sb(c_sc, f"tmpf{seq}", [128, 2048], F32)
        xnt2 = tmpf

        ao_reads = [("AO", h, qb) for h in range(NH) for qb in range(4)]
        for q in range(4):
            p.g("gpsimd", [
                (lambda e, q=q: e.dma_start(out=WOr[:, :], in_=wo_d[q]), 16),
            ], writes=[("WOr",)], dma=True)
            pe_settle([("WOr",)])
            for tc in range(TC):
                bank = PS[tc % 4]
                ops = []
                for dinc in range(16):
                    ops.append((mm(bank[:, :],
                                   AO[:, dinc * 2048 + tc * 128: dinc * 2048 + (tc + 1) * 128],
                                   WOr[:, dinc * 512:(dinc + 1) * 512],
                                   dinc == 0, dinc == 15), 0))
                ops[-1] = (ops[-1][0], 1)
                p.g("tensor", ops, reads=ao_reads + [("WOr",)], writes=[("ps", tc % 4)])
                sl = tc % 2
                p.g("scalar", [
                    (lambda e, sl=sl, bank=bank: e.activation(
                        ytile[sl][:, :], bank[:, :], AF.Copy, bias=0.0, scale=1.0), 1),
                ], reads=[("ps", tc % 4)], writes=[("ytile", sl)])
                p.g("sync", [
                    (lambda e, sl=sl, tc=tc, q=q: e.dma_start(
                        out=ys[tc * 128:(tc + 1) * 128, q * 512:(q + 1) * 512],
                        in_=ytile[sl][:, :]), 16),
                ], reads=[("ytile", sl)], writes=[("ys", tc, q)], dma=True)
        for tc in range(TC):
            sl = tc % 2
            yf = yfull[0]
            xc = xch[0]
            x2 = x2t[sl]
            p.g("sync", [
                (lambda e, yf=yf, tc=tc: e.dma_start(out=yf[:, :], in_=ys[tc * 128:(tc + 1) * 128, :]), 16),
                (lambda e, xc=xc, tc=tc, seq=seq: e.dma_start(
                    out=xc[:, :], in_=x_in[seq, tc * 128:(tc + 1) * 128, :]), 16),
            ], reads=[("ys", tc, 0), ("ys", tc, 1), ("ys", tc, 2), ("ys", tc, 3)],
                writes=[("yfull", 0), ("xch", 0)], dma=True)
            c0 = 4 * sl
            plan_rowln(yf, ("yfull", 0), tmpf, ("tmpf",), c0, ("c", sl))
            p.g("vector", [
                (lambda e, x2=x2, xc=xc, c0=c0: e.scalar_tensor_tensor(
                    x2[:, :], tmpf[:, :], stats[:, c0 + 3:c0 + 4], xc[:, :],
                    ALU.mult, ALU.add), 1),
            ], reads=[("tmpf",), ("xch", 0), ("stats", ("c", sl))], writes=[("x2t", sl)])
            p.g("sync", [
                (lambda e, x2=x2, tc=tc: e.dma_start(out=x2s[tc * 128:(tc + 1) * 128, :], in_=x2[:, :]), 16),
            ], reads=[("x2t", sl)], writes=[("x2s", tc)], dma=True)
            plan_rowln(x2, ("x2t", sl), tmpf, ("tmpf",), c0, ("c", sl))
            p.g("vector", [
                (lambda e, yf=yf, c0=c0: e.tensor_scalar(
                    yf[:, :], tmpf[:, :], stats[:, c0 + 3:c0 + 4], 1.0,
                    ALU.mult, ALU.mult), 1),
            ], reads=[("tmpf",), ("stats", ("c", sl))], writes=[("yfull", 0)])
            for g4 in range(4):
                bank = PS[4 + g4 % 2]
                tr_ops = []
                for j in range(4):
                    dinc = g4 * 4 + j
                    tr_ops.append((lambda e, yf=yf, dinc=dinc, j=j, bank=bank: e.transpose(
                        bank[:, j * 128:(j + 1) * 128],
                        yf[:, dinc * 128:(dinc + 1) * 128], ident[:, :]), 0))
                tr_ops[-1] = (tr_ops[-1][0], 1)
                p.g("tensor", tr_ops, reads=[("yfull", 0), ("ident",)], writes=[("ps", 4 + g4 % 2)])
                xnap = XN[:, :].rearrange("p (dinc t) -> p dinc t", dinc=16)
                p.g("scalar", [
                    (lambda e, g4=g4, tc=tc, bank=bank, xnap=xnap: e.activation(
                        xnap[:, g4 * 4:g4 * 4 + 4, tc * 128:(tc + 1) * 128],
                        bank[:, :], AF.Copy, bias=0.0, scale=1.0), 1),
                ], reads=[("ps", 4 + g4 % 2)], writes=[("XN", tc)])
        if DBG and seq == 0:
            p.g("sync", [(lambda e: e.dma_start(out=dxn2, in_=XN[:, :]), 16)],
                reads=[("XN", t) for t in range(TC)], dma=True)
        close_scope(c_sc)
        close_scope(attn_sc)
        p.barrier()

        ffn_sc = mk_scope()
        WG = [sb(ffn_sc, f"WG{seq}_{i}", [128, 2048], BF16) for i in range(2)]
        WU = [sb(ffn_sc, f"WU{seq}_{i}", [128, 2048], BF16) for i in range(2)]
        GS = [sb(ffn_sc, f"GS{seq}_{i}", [128, 512], BF16) for i in range(2)]
        HST = [sb(ffn_sc, f"HST{seq}_{i}", [128, 512], BF16) for i in range(2)]
        HBLK = sb(ffn_sc, f"HBLK{seq}", [128, ICC * 512], BF16)
        WDc = [sb(ffn_sc, f"WDc{seq}_{i}", [128, 1024], BF16) for i in range(2)]
        ybl = [sb(ffn_sc, f"ybl{seq}_{i}", [128, 2048], F32) for i in range(4)]
        x2e = [sb(ffn_sc, f"x2e{seq}_0", [128, 2048], F32)]
        tmpe = sb(ffn_sc, f"tmpe{seq}", [128, 2048], F32)

        for icc in range(ICC):
            sl = icc % 2
            p.g("sync", [
                (lambda e, sl=sl, icc=icc: e.dma_start(
                    out=WG[sl][:, :], in_=wall_ap[icc * 128:(icc + 1) * 128, :]), 16),
                (lambda e, sl=sl, icc=icc: e.dma_start(
                    out=WU[sl][:, :], in_=wall_ap[(ICC + icc) * 128:(ICC + icc + 1) * 128, :]), 16),
            ], reads=[("wall",)], writes=[("WG", sl), ("WU", sl)], dma=True)
            if icc < 2:
                pe_settle([("WG", sl), ("WU", sl)])
            for tb in range(4):
                bgi = (2 * tb) % 6
                bui = (2 * tb + 1) % 6
                bg = PS[bgi]
                bu = PS[bui]
                for (w, wkey, bank, bid) in ((WG[sl], "WG", bg, bgi), (WU[sl], "WU", bu, bui)):
                    ops = []
                    for dinc in range(16):
                        ops.append((mm(bank[:, :],
                                       w[:, dinc * 128:(dinc + 1) * 128],
                                       XN[:, dinc * 2048 + tb * 512: dinc * 2048 + (tb + 1) * 512],
                                       dinc == 0, dinc == 15), 0))
                    ops[-1] = (ops[-1][0], 1)
                    p.g("tensor", ops, reads=xn_reads + [(wkey, sl)], writes=[("ps", bid)])
                ssl = tb % 2
                p.g("scalar", [
                    (lambda e, ssl=ssl, bg=bg: e.activation(GS[ssl][:, :], bg[:, :], AF.Silu,
                                                            bias=0.0, scale=1.0), 1),
                ], reads=[("ps", bgi)], writes=[("GS", ssl)])
                p.g("vector", [
                    (lambda e, ssl=ssl, bu=bu: e.scalar_tensor_tensor(
                        HST[ssl][:, :], bu[:, :], 1.0, GS[ssl][:, :], ALU.mult, ALU.mult), 1),
                ], reads=[("ps", bui), ("GS", ssl)], writes=[("HST", ssl)])
                p.g("sync", [
                    (lambda e, ssl=ssl, tb=tb, icc=icc: e.dma_start(
                        out=hs[tb * 128:(tb + 1) * 128, icc * 512:(icc + 1) * 512],
                        in_=HST[ssl][:, :]), 16),
                ], reads=[("HST", ssl)], writes=[("hs", icc, tb)], dma=True)
        p.barrier()

        for tb in range(4):
            hs_reads = [("hs", icc, tb) for icc in range(ICC)]
            p.g("sync", [
                (lambda e, tb=tb: e.dma_start(out=HBLK[:, :],
                                              in_=hs[tb * 128:(tb + 1) * 128, :]), 16),
            ], reads=hs_reads, writes=[("HBLK",)], dma=True)
            pe_settle([("HBLK",)], n=24)
            for half in range(2):
                for qp in range(2):
                    bmap = [0, 1, 2, 3] if (half * 2 + qp) % 2 == 0 else [4, 5, 6, 0]
                    for icc in range(ICC):
                        sl = icc % 2
                        dblk = 2 * ICC + icc
                        p.g("sync", [
                            (lambda e, sl=sl, dblk=dblk, half=half: e.dma_start(
                                out=WDc[sl][:, :],
                                in_=wall_ap[dblk * 128:(dblk + 1) * 128,
                                            half * 1024:(half + 1) * 1024]), 16),
                        ], reads=[("wall",)], writes=[("WDc", sl)], dma=True)
                        if icc < 2:
                            pe_settle([("WDc", sl)])
                        ops_by_bank = []
                        for ti in range(2):
                            tcc = 2 * qp + ti
                            for nb in range(2):
                                bid = bmap[ti * 2 + nb]
                                ops_by_bank.append((mm(
                                    PS[bid][:, :],
                                    HBLK[:, icc * 512 + tcc * 128: icc * 512 + (tcc + 1) * 128],
                                    WDc[sl][:, nb * 512:(nb + 1) * 512],
                                    icc == 0, icc == ICC - 1),
                                    1 if ti == 1 and nb == 1 else 0))
                        p.g("tensor", ops_by_bank,
                            reads=[("HBLK",), ("WDc", sl)],
                            writes=[("ps", i) for i in bmap])
                    for ti in range(2):
                        tcc = 2 * qp + ti
                        for nb in range(2):
                            bid = bmap[ti * 2 + nb]
                            p.g("scalar", [
                                (lambda e, tcc=tcc, nb=nb, half=half, bid=bid: e.activation(
                                    ybl[tcc][:, half * 1024 + nb * 512: half * 1024 + (nb + 1) * 512],
                                    PS[bid][:, :], AF.Copy, bias=0.0, scale=1.0), 1),
                            ], reads=[("ps", bid)], writes=[("ybl", tcc)])
            if DBG and seq == 1:
                p.g("sync", [(lambda e, tb=tb: e.dma_start(out=dybl[tb], in_=ybl[0][:, :]), 16)],
                    reads=[("ybl", 0)], dma=True)
            for tcc in range(4):
                tc = tb * 4 + tcc
                sl = tcc % 2
                x2 = x2e[0]
                p.g("sync", [
                    (lambda e, x2=x2, tc=tc: e.dma_start(
                        out=x2[:, :], in_=x2s[tc * 128:(tc + 1) * 128, :]), 16),
                ], reads=[("x2s", tc)], writes=[("x2e", 0)], dma=True)
                if DBG and seq == 1 and tcc == 0:
                    p.g("sync", [(lambda e, tb=tb, x2=x2: e.dma_start(out=dx2e[tb], in_=x2[:, :]), 16)],
                        reads=[("x2e", 0)], dma=True)
                c0 = 4 * sl
                plan_rowln(ybl[tcc], ("ybl", tcc), tmpe, ("tmpe",), c0, ("e", sl))
                p.g("vector", [
                    (lambda e, tcc=tcc, x2=x2, c0=c0: e.scalar_tensor_tensor(
                        ybl[tcc][:, :], tmpe[:, :], stats[:, c0 + 3:c0 + 4], x2[:, :],
                        ALU.mult, ALU.add), 1),
                ], reads=[("tmpe",), ("x2e", 0), ("stats", ("e", sl))], writes=[("ybl", tcc)])
                p.g("sync", [
                    (lambda e, tcc=tcc, tc=tc, seq=seq: e.dma_start(
                        out=y_out[seq, tc * 128:(tc + 1) * 128, :], in_=ybl[tcc][:, :]), 16),
                ], reads=[("ybl", tcc)], writes=[("yout", seq, tc)], dma=True)
        close_scope(ffn_sc)
        p.barrier()

    final_sync = p.cnt["syncq"]
    p.steps["sync"].append(("wait", "syncq", final_sync))

    semh = {}
    for name in ("pe", "dve", "act", "pool", "syncq", "cc"):
        gg = nc.semaphore(name)
        semh[name] = gg.__enter__()
        all_ctx.append(gg)

    blk = nc.Block()
    block = blk.__enter__()

    @block.tensor
    def _(eng):
        p.emit("tensor", eng, semh)

    @block.vector
    def _(eng):
        p.emit("vector", eng, semh)

    @block.scalar
    def _(eng):
        p.emit("scalar", eng, semh)

    @block.gpsimd
    def _(eng):
        p.emit("gpsimd", eng, semh)

    @block.sync
    def _(eng):
        p.emit("sync", eng, semh)

    blk.__exit__(None, None, None)
    for gg in reversed(all_ctx):
        gg.__exit__(None, None, None)
    close_scope(persist)
    return nc


def _prep_inputs(L0_attn_x, L0_attn_wq, L0_attn_wk, L0_attn_wv, L0_attn_wo,
                 L0_ffn_lut, L0_ffn_gate_idx, L0_ffn_up_idx, L0_ffn_down_idx):
    x = np.ascontiguousarray(np.asarray(L0_attn_x, dtype=np.float32))
    wq = np.asarray(L0_attn_wq, dtype=np.float32)
    wk = np.asarray(L0_attn_wk, dtype=np.float32)
    wv = np.asarray(L0_attn_wv, dtype=np.float32)
    wo = np.asarray(L0_attn_wo, dtype=np.float32)
    lut = np.ascontiguousarray(np.asarray(L0_ffn_lut, dtype=np.float32))
    gidx = np.asarray(L0_ffn_gate_idx, dtype=np.int32)
    uidx = np.asarray(L0_ffn_up_idx, dtype=np.int32)
    didx = np.asarray(L0_ffn_down_idx, dtype=np.int32)

    wq_t = np.ascontiguousarray(wq.reshape(16, 128, 16, 128).transpose(0, 3, 2, 1))
    wk_t = np.ascontiguousarray(wk.reshape(16, 128, 16, 128).transpose(0, 3, 2, 1))
    wv_t = np.ascontiguousarray(wv.reshape(8, 256, 16, 128).transpose(0, 3, 2, 1))
    wo_t = np.ascontiguousarray(wo.reshape(4, 512, 16, 128).transpose(0, 3, 2, 1))

    wg_wall = gidx.T.reshape(16, 128, ICC, 128).transpose(2, 1, 0, 3).reshape(ICC * 128, 2048)
    wu_wall = uidx.T.reshape(16, 128, ICC, 128).transpose(2, 1, 0, 3).reshape(ICC * 128, 2048)
    wd_wall = didx.T.reshape(ICC, 128, 16, 128).reshape(ICC * 128, 2048)
    wallidx = np.ascontiguousarray(np.concatenate([wg_wall, wu_wall, wd_wall], axis=0))

    ident = np.eye(128, dtype=np.float32)
    in_maps = []
    for c in range(NCORES):
        in_maps.append({
            "x": np.ascontiguousarray(x[c * SEQ_PER_CORE:(c + 1) * SEQ_PER_CORE]),
            "wq_t": wq_t, "wk_t": wk_t, "wv_t": wv_t, "wo_t": wo_t,
            "lut": lut,
            "idx_shard": np.ascontiguousarray(wallidx[c * SHARD_ROWS:(c + 1) * SHARD_ROWS]),
            "ident": ident,
        })
    return in_maps


def kernel(L0_attn_x, L0_attn_wq, L0_attn_wk, L0_attn_wv, L0_attn_wo,
           L0_ffn_lut, L0_ffn_gate_idx, L0_ffn_up_idx, L0_ffn_down_idx):
    in_maps = _prep_inputs(L0_attn_x, L0_attn_wq, L0_attn_wk, L0_attn_wv, L0_attn_wo,
                           L0_ffn_lut, L0_ffn_gate_idx, L0_ffn_up_idx, L0_ffn_down_idx)
    nc = build_nc()
    res = run_bass_kernel_spmd(nc, in_maps, core_ids=list(range(NCORES)))
    out = np.concatenate([res.results[c]["y"] for c in range(NCORES)], axis=0)
    return out
